# version 02
# baseline (speedup 1.0000x reference)
import contextlib
import math
import numpy as np
import ml_dtypes
import concourse.bass as bass
import concourse.mybir as mybir
from concourse.bass_utils import run_bass_kernel_spmd

F32 = mybir.dt.float32
BF16 = mybir.dt.bfloat16
I32 = mybir.dt.int32
AF = mybir.ActivationFunctionType
ALU = mybir.AluOpType
AX = mybir.AxisListType

S = 4096
E = 4096
Bsz = 4
OWN = 2048
INV_SQRT_D = 1.0 / math.sqrt(128.0)
EPS = 1e-5

FP4 = np.array([0.0, 0.0052, 0.6667, 1.0, 0.3333, 0.5, 0.1667, 0.25,
                0.0, -0.0052, -0.6667, -1.0, -0.3333, -0.5, -0.1667, -0.25],
               dtype=np.float32)

DEBUG_OUTPUTS = False


def _poly_coeffs():
    m = np.arange(8, dtype=np.float64)
    V = np.vander(m, 8, increasing=True)
    p = np.linalg.solve(V[1:, 1:8], FP4[0:8][1:].astype(np.float64))
    return dict(a=float(p[6]), b=float(p[5]),
                c=[0.0, float(p[4]), float(p[3]), float(p[2]), float(p[1]), float(p[0])])


CO = _poly_coeffs()


class Sem:
    def __init__(self, es, nc, name):
        self.h = es.enter_context(nc.semaphore(name))
        self.n = 0

    def inc(self, instr, k=1):
        instr.then_inc(self.h, k)
        self.n += k
        return self.n


def own_row_index(h):
    rows = []
    for g in range(4):
        rows.extend(range(1024 * g + 512 * h, 1024 * g + 512 * h + 512))
    return np.array(rows)



def build_program():
    nc = bass.Bass("TRN2", target_bir_lowering=False)

    x_full = nc.dram_tensor("x_full", [S, E], F32, kind="ExternalInput").ap()
    x_own = nc.dram_tensor("x_own", [OWN, E], F32, kind="ExternalInput").ap()
    winq_t = nc.dram_tensor("winq_t", [32, E, 128], F32, kind="ExternalInput").ap()
    winkv_t = nc.dram_tensor("winkv_t", [16, E, 128], F32, kind="ExternalInput").ap()
    woutT = nc.dram_tensor("woutT", [E, E], F32, kind="ExternalInput").ap()
    pkT = nc.dram_tensor("pkT", [2048, 4096], I32, kind="ExternalInput").ap()
    srep = nc.dram_tensor("srep", [2048, 4096], F32, kind="ExternalInput").ap()
    biasbc = nc.dram_tensor("biasbc", [128, E], F32, kind="ExternalInput").ap()
    ck_c = nc.dram_tensor("ck_c", [128, S], F32, kind="ExternalInput").ap()
    ck_s2 = nc.dram_tensor("ck_s2", [128, S], F32, kind="ExternalInput").ap()
    cq_c = nc.dram_tensor("cq_c", [128, OWN], F32, kind="ExternalInput").ap()
    cq_s2 = nc.dram_tensor("cq_s2", [128, OWN], F32, kind="ExternalInput").ap()
    masks = nc.dram_tensor("masks", [8, 128, 512], BF16, kind="ExternalInput").ap()

    out_own = nc.dram_tensor("out_own", [OWN, E], F32, kind="ExternalOutput").ap()

    ln1x_sp = nc.dram_tensor("ln1x_sp", [S, E], BF16, kind="Internal").ap()
    ln1xq_sp = nc.dram_tensor("ln1xq_sp", [OWN, E], BF16, kind="Internal").ap()
    kf_sp = nc.dram_tensor("kf_sp", [1024, S], BF16, kind="Internal").ap()
    vf_sp = nc.dram_tensor("vf_sp", [1024, S], BF16, kind="Internal").ap()
    qf_sp = nc.dram_tensor("qf_sp", [E, OWN], BF16, kind="Internal").ap()
    avf_sp = nc.dram_tensor("avf_sp", [E, OWN], BF16, kind="Internal").ap()
    r1_sp = nc.dram_tensor("r1_sp", [OWN, E], F32, kind="Internal").ap()
    ln2x_sp = nc.dram_tensor("ln2x_sp", [OWN, E], BF16, kind="Internal").ap()
    wt_ffn = nc.dram_tensor("wt_ffn", [E, E], BF16, kind="Internal").ap()
    wout_bf = nc.dram_tensor("wout_bf", [E, E], BF16, kind="Internal").ap()
    at_hi_sp = nc.dram_tensor("at_hi_sp", [E, OWN], BF16, kind="Internal").ap()
    at_lo_sp = nc.dram_tensor("at_lo_sp", [E, OWN], BF16, kind="Internal").ap()
    ff_hi_sp = nc.dram_tensor("ff_hi_sp", [E, OWN], BF16, kind="Internal").ap()
    ff_lo_sp = nc.dram_tensor("ff_lo_sp", [E, OWN], BF16, kind="Internal").ap()

    dbg = {}
    if DEBUG_OUTPUTS:
        for nm, shp, dt_ in (("kf", [1024, S], BF16), ("vf", [1024, S], BF16),
                             ("qf", [E, OWN], BF16), ("avf", [E, OWN], BF16),
                             ("r1", [OWN, E], F32), ("wt", [E, E], BF16),
                             ("ln2", [OWN, E], BF16), ("ln1", [S, E], BF16),
                             ("wob", [E, E], BF16), ("dtmp0", [128, 2048], F32),
                             ("r1b0", [128, E], BF16), ("fsl0", [128, 2048], BF16),
                             ("l2f0", [128, 2048], BF16)):
            dbg[nm] = nc.dram_tensor("dbg_" + nm, shp, dt_, kind="ExternalOutput").ap()

    es_g = contextlib.ExitStack()
    with es_g:
        ones_l = es_g.enter_context(nc.sbuf_tensor("ones_l", [128, 1], BF16))
        ones_b = es_g.enter_context(nc.sbuf_tensor("ones_b", [1, 128], BF16))

        sems = {}
        for nm in ["qs", "qa", "qg", "cv", "ca", "ct", "cg", "vself", "aself", "st"]:
            sems[nm] = Sem(es_g, nc, nm)
        qs, qa, qg = sems["qs"], sems["qa"], sems["qg"]
        cv, ca, ct, cg = sems["cv"], sems["ca"], sems["ct"], sems["cg"]
        vself, aself = sems["vself"], sems["aself"]
        st = sems["st"]

        block = es_g.enter_context(nc.Block())

        @block.gpsimd
        def _(g):
            g.memset(ones_l[:], 1.0)
            cg.inc(g.memset(ones_b[:], 1.0))

        def full_barrier():
            vals = [(qs, qs.n), (qa, qa.n), (qg, qg.n), (cv, cv.n), (ca, ca.n),
                    (ct, ct.n), (cg, cg.n)]

            @block.sync
            def _(sync):
                for sem, v in vals:
                    sync.wait_ge(sem.h, v)

            @block.scalar
            def _(sc):
                for sem, v in vals:
                    sc.wait_ge(sem.h, v)

            @block.vector
            def _(v_):
                for sem, v in vals:
                    v_.wait_ge(sem.h, v)

            @block.gpsimd
            def _(g):
                for sem, v in vals:
                    g.wait_ge(sem.h, v)

            @block.tensor
            def _(t):
                for sem, v in vals:
                    t.wait_ge(sem.h, v)

        with contextlib.ExitStack() as es_w:
            wcvt = [es_w.enter_context(nc.sbuf_tensor(f"wcvt{i}", [128, E], BF16))
                    for i in range(2)]
            wcvt_ld = []
            wcvt_st = []

            @block.gpsimd
            def _(g):
                for i in range(32):
                    par = i % 2
                    if i >= 2:
                        g.wait_ge(qg.h, wcvt_st[i - 2])
                    wcvt_ld.append(qg.inc(g.dma_start(
                        out=wcvt[par][:], in_=woutT[i * 128:(i + 1) * 128, :]), 16))
                    g.wait_ge(qg.h, wcvt_ld[i])
                    wcvt_st.append(qg.inc(g.dma_start(
                        out=wout_bf[i * 128:(i + 1) * 128, :], in_=wcvt[par][:]), 16))
            woutbf_done = qg.n
        full_barrier()

        DQW = 256
        NSLAB = 4096 // DQW
        wt_slab_ready = {}
        dqb = {}
        dq_state = dict(idx=0, total=16 * NSLAB, pk_loaded=[], hi_done=[], lo_done=[],
                        st_h=[], st_l=[], extract=[])

        def alloc_dq(es):
            for nmv, dt_ in (("pk", I32), ("sr", F32), ("hii", I32), ("loi", I32),
                             ("nfh", F32), ("nfl", F32), ("gh", F32), ("gl", F32),
                             ("sh", F32), ("sl", F32), ("t0h", F32), ("t0l", F32),
                             ("t1h", F32), ("t1l", F32)):
                dqb[nmv] = es.enter_context(
                    nc.sbuf_tensor("dq" + nmv, [128, DQW], dt_))
            dqb["oh"] = es.enter_context(nc.sbuf_tensor("dqoh", [128, DQW], BF16))
            dqb["ol"] = es.enter_context(nc.sbuf_tensor("dqol", [128, DQW], BF16))

        def emit_dq(nslabs):
            for _ in range(nslabs):
                i = dq_state["idx"]
                if i >= dq_state["total"]:
                    return
                b, sl = i // NSLAB, i % NSLAB
                c0 = sl * DQW

                @block.scalar
                def _(sc):
                    if i >= 1:
                        sc.wait_ge(cv.h, dq_state["extract"][i - 1])
                        sc.wait_ge(cv.h, dq_state["hi_done"][i - 1])
                        sc.wait_ge(cg.h, dq_state["lo_done"][i - 1])
                    qa.inc(sc.dma_start(out=dqb["pk"][:],
                                        in_=pkT[b * 128:(b + 1) * 128, c0:c0 + DQW]), 16)
                    qa.inc(sc.dma_start(out=dqb["sr"][:],
                                        in_=srep[b * 128:(b + 1) * 128, c0:c0 + DQW]), 16)
                dq_state["pk_loaded"].append(qa.n)

                @block.vector
                def _(v):
                    v.wait_ge(qa.h, dq_state["pk_loaded"][i])
                    if i >= 1:
                        v.wait_ge(cg.h, dq_state["lo_done"][i - 1])
                        v.wait_ge(qa.h, dq_state["st_h"][i - 1])
                    v.tensor_scalar(dqb["hii"][:], dqb["pk"][:], 4, 15,
                                    ALU.logical_shift_right, ALU.bitwise_and)
                    v.tensor_single_scalar(dqb["loi"][:], dqb["pk"][:], 15,
                                           ALU.bitwise_and)
                    v.tensor_copy(dqb["nfh"][:], dqb["hii"][:])
                    cv.inc(v.tensor_copy(dqb["nfl"][:], dqb["loi"][:]))
                dq_state["extract"].append(cv.n)

                @block.vector
                def _(v):
                    v.tensor_single_scalar(dqb["gh"][:], dqb["nfh"][:], 8.0, ALU.is_ge)
                    v.tensor_scalar(dqb["sh"][:], dqb["gh"][:], -2.0, 1.0,
                                    ALU.mult, ALU.add)
                    v.scalar_tensor_tensor(dqb["t0h"][:], dqb["gh"][:], -8.0,
                                           dqb["nfh"][:], ALU.mult, ALU.add)
                    v.tensor_scalar(dqb["t1h"][:], dqb["t0h"][:], CO["a"], CO["b"],
                                    ALU.mult, ALU.add)
                    cur, oth = "t1h", "gh"
                    for cj in CO["c"]:
                        v.scalar_tensor_tensor(dqb[oth][:], dqb[cur][:], cj,
                                               dqb["t0h"][:], ALU.add, ALU.mult)
                        cur, oth = oth, cur
                    vself.inc(v.tensor_tensor(dqb["nfh"][:], dqb[cur][:], dqb["sh"][:],
                                              ALU.mult))
                    v.wait_ge(vself.h, vself.n)
                    cv.inc(v.tensor_tensor(dqb["oh"][:], dqb["nfh"][:], dqb["sr"][:],
                                           ALU.mult))
                dq_state["hi_done"].append(cv.n)

                @block.gpsimd
                def _(g):
                    g.wait_ge(cv.h, dq_state["extract"][i])
                    if i >= 1:
                        g.wait_ge(qa.h, dq_state["st_l"][i - 1])
                    g.tensor_single_scalar(dqb["gl"][:], dqb["nfl"][:], 8.0, ALU.is_ge)
                    g.tensor_scalar(dqb["sl"][:], dqb["gl"][:], -2.0, 1.0,
                                    ALU.mult, ALU.add)
                    g.tensor_single_scalar(dqb["t0l"][:], dqb["gl"][:], 8.0, ALU.mult)
                    g.tensor_tensor(dqb["t0l"][:], dqb["nfl"][:], dqb["t0l"][:],
                                    ALU.subtract)
                    g.tensor_scalar(dqb["t1l"][:], dqb["t0l"][:], CO["a"], CO["b"],
                                    ALU.mult, ALU.add)
                    cur, oth = "t1l", "gl"
                    for cj in CO["c"]:
                        g.tensor_single_scalar(dqb[oth][:], dqb[cur][:], cj, ALU.add)
                        g.tensor_tensor(dqb[oth][:], dqb[oth][:], dqb["t0l"][:],
                                        ALU.mult)
                        cur, oth = oth, cur
                    g.tensor_tensor(dqb["nfl"][:], dqb[cur][:], dqb["sl"][:], ALU.mult)
                    cg.inc(g.tensor_tensor(dqb["ol"][:], dqb["nfl"][:], dqb["sr"][:],
                                           ALU.mult))
                dq_state["lo_done"].append(cg.n)

                @block.scalar
                def _(sc):
                    sc.wait_ge(cv.h, dq_state["hi_done"][i])
                    dq_state["st_h"].append(qa.inc(sc.dma_start(
                        out=wt_ffn[256 * b:256 * b + 128, c0:c0 + DQW],
                        in_=dqb["oh"][:]), 16))
                    sc.wait_ge(cg.h, dq_state["lo_done"][i])
                    dq_state["st_l"].append(qa.inc(sc.dma_start(
                        out=wt_ffn[256 * b + 128:256 * b + 256, c0:c0 + DQW],
                        in_=dqb["ol"][:]), 16))
                if sl == NSLAB - 1:
                    wt_slab_ready[2 * b] = dq_state["st_h"][-1]
                    wt_slab_ready[2 * b + 1] = dq_state["st_l"][-1]
                dq_state["idx"] += 1

        def finish_dq():
            while dq_state["idx"] < dq_state["total"]:
                emit_dq(4)


        def ln_pass(es, tag, src_x, n_rows, spill_ln, perm_out=False):
            nt = n_rows // 128
            xa = [es.enter_context(nc.sbuf_tensor(f"xa{tag}{i}", [128, E], F32))
                  for i in range(2)]
            x2s = es.enter_context(nc.sbuf_tensor(f"x2s{tag}", [128, E], F32))
            lnst = [es.enter_context(nc.sbuf_tensor(f"lnst{tag}{i}", [128, E], BF16))
                    for i in range(2)]
            stt2 = [es.enter_context(nc.sbuf_tensor(f"stt{tag}{i}", [128, 6], F32))
                    for i in range(2)]

            loads = []
            spills = []
            norm_done = []

            @block.sync
            def _(sync):
                for i in range(min(2, nt)):
                    loads.append(qs.inc(sync.dma_start(
                        out=xa[i][:], in_=src_x[i * 128:(i + 1) * 128, :]), 16))

            for i in range(nt):
                par = i % 2
                stt = stt2[par]

                @block.vector
                def _(v):
                    v.wait_ge(qs.h, loads[i])
                    if i >= 2:
                        v.wait_ge(ca.h, norm_done[i - 2])
                    v.reduce_sum(stt[:, 0:1], xa[par][:], axis=AX.X)
                    v.tensor_tensor(x2s[:], xa[par][:], xa[par][:], ALU.mult)
                    vself.inc(v.reduce_sum(stt[:, 1:2], x2s[:], axis=AX.X))
                    v.wait_ge(vself.h, vself.n)
                    v.tensor_scalar(stt[:, 2:3], stt[:, 0:1], 1.0 / E, None, ALU.mult)
                    vself.inc(v.tensor_scalar(stt[:, 3:4], stt[:, 1:2], 1.0 / E, EPS,
                                              ALU.mult, ALU.add))
                    v.wait_ge(vself.h, vself.n)
                    vself.inc(v.tensor_tensor(x2s[:, 0:1], stt[:, 2:3], stt[:, 2:3],
                                              ALU.mult))
                    v.wait_ge(vself.h, vself.n)
                    cv.inc(v.tensor_tensor(stt[:, 3:4], stt[:, 3:4], x2s[:, 0:1],
                                           ALU.subtract))

                @block.scalar
                def _(sc):
                    sc.wait_ge(cv.h, cv.n)
                    aself.inc(sc.activation(out=stt[:, 4:5], in_=stt[:, 3:4], func=AF.Ln))
                    sc.wait_ge(aself.h, aself.n)
                    ca.inc(sc.activation(out=stt[:, 4:5], in_=stt[:, 4:5], func=AF.Exp,
                                         scale=-0.5))

                @block.vector
                def _(v):
                    v.wait_ge(ca.h, ca.n)
                    vself.inc(v.tensor_tensor(stt[:, 5:6], stt[:, 2:3], stt[:, 4:5],
                                              ALU.mult))
                    v.wait_ge(vself.h, vself.n)
                    cv.inc(v.tensor_scalar(stt[:, 5:6], stt[:, 5:6], -1.0, None,
                                           ALU.mult))

                @block.scalar
                def _(sc):
                    sc.wait_ge(cv.h, cv.n)
                    if i >= 2:
                        sc.wait_ge(qa.h, spills[i - 2])
                    if perm_out:
                        norm_done.append(ca.inc(sc.activation(
                            out=lnst[par][:].rearrange("p (w h j) -> p w h j",
                                                       w=16, h=2, j=128),
                            in_=xa[par][:].rearrange("p (w j h) -> p w h j",
                                                     w=16, j=128, h=2),
                            func=AF.Identity, bias=stt[:, 5:6], scale=stt[:, 4:5])))
                    else:
                        norm_done.append(ca.inc(sc.activation(
                            out=lnst[par][:], in_=xa[par][:], func=AF.Identity,
                            bias=stt[:, 5:6], scale=stt[:, 4:5])))
                    sc.wait_ge(ca.h, norm_done[i])
                    spills.append(qa.inc(sc.dma_start(
                        out=spill_ln[i * 128:(i + 1) * 128, :], in_=lnst[par][:]), 16))

                if i + 2 < nt:
                    @block.sync
                    def _(sync):
                        sync.wait_ge(ca.h, norm_done[i])
                        loads.append(qs.inc(sync.dma_start(
                            out=xa[par][:], in_=src_x[(i + 2) * 128:(i + 3) * 128, :]),
                            16))
                emit_dq(1)

        def proj_pass(es, tag, spill_ln, row0, n_rows, of_specs, cs_pair):
            ncols = n_rows
            lnxf = es.enter_context(nc.sbuf_tensor(f"lnxf{tag}", [128, 32 * ncols], BF16))

            @block.sync
            def _(sync):
                sync.wait_ge(qa.h, qa.n)
                for j in range(32):
                    qs.inc(sync.dma_start(
                        out=lnxf[:, j * ncols:(j + 1) * ncols],
                        in_=spill_ln[row0:row0 + n_rows, j * 128:(j + 1) * 128],
                        transpose=True), 16)
            tload_done = qs.n

            wof = [es.enter_context(nc.sbuf_tensor(f"wof{tag}{i}", [128, E], BF16))
                   for i in range(2)]
            ev = [es.enter_context(nc.sbuf_tensor(f"ev{tag}{i}", [128, 1024], F32))
                  for i in range(2)]
            evsw = [es.enter_context(nc.sbuf_tensor(f"evs{tag}{i}", [128, 1024], F32))
                    for i in range(2)]
            t1 = es.enter_context(nc.sbuf_tensor(f"t1{tag}", [128, 1024], F32))
            t2 = es.enter_context(nc.sbuf_tensor(f"t2{tag}", [128, 1024], F32))
            kst = [es.enter_context(nc.sbuf_tensor(f"kst{tag}{i}", [128, 1024], BF16))
                   for i in range(2)]
            vstg = es.enter_context(nc.sbuf_tensor(f"vst{tag}", [128, ncols], BF16))
            pkps = [es.enter_context(nc.psum_tensor(f"pk{tag}{i}", [128, ncols], F32))
                    for i in range(2)]

            wloads = []
            mm_done = []
            evac_done = []
            rope_cv = []
            rope_qs = []
            rope_qa = []
            vstg_qa = []

            @block.gpsimd
            def _(g):
                wloads.append(qg.inc(g.dma_start(
                    out=wof[0][:].rearrange("p (g o) -> p g o", g=32),
                    in_=of_specs[0]["w_src"].rearrange("(g p) o -> p g o", p=128)), 16))

            r_idx = 0
            for oi, spec in enumerate(of_specs):
                par = oi % 2

                if oi + 1 < len(of_specs):
                    @block.gpsimd
                    def _(g):
                        if oi >= 1:
                            g.wait_ge(ct.h, mm_done[oi - 1])
                        wloads.append(qg.inc(g.dma_start(
                            out=wof[(oi + 1) % 2][:].rearrange("p (g o) -> p g o", g=32),
                            in_=of_specs[oi + 1]["w_src"].rearrange(
                                "(g p) o -> p g o", p=128)), 16))

                @block.tensor
                def _(t):
                    t.wait_ge(qg.h, wloads[oi])
                    t.wait_ge(qs.h, tload_done)
                    if oi >= 2:
                        sem, val = evac_done[oi - 2]
                        t.wait_ge(sem.h, val)
                    ins = None
                    for ig in range(32):
                        for kx in range(ncols // 512):
                            ins = t.matmul(
                                pkps[par][:, kx * 512:(kx + 1) * 512],
                                wof[par][:, ig * 128:(ig + 1) * 128],
                                lnxf[:, ig * ncols + kx * 512: ig * ncols + (kx + 1) * 512],
                                start=(ig == 0), stop=(ig == 31))
                    mm_done.append(ct.inc(ins))

                if spec["kind"] == "v":
                    @block.scalar
                    def _(sc):
                        sc.wait_ge(ct.h, mm_done[oi])
                        sc.nop(cycle_cnt=2048)
                        if vstg_qa:
                            sc.wait_ge(qa.h, vstg_qa[-1])
                        ca.inc(sc.copy(vstg[:], pkps[par][:]))
                        sc.wait_ge(ca.h, ca.n)
                        vstg_qa.append(qa.inc(sc.dma_start(
                            out=spec["dst"][:, spec["dst_col0"]:spec["dst_col0"] + ncols],
                            in_=vstg[:]), 16))
                    evac_done.append((ca, ca.n))
                else:
                    cf, sf2 = cs_pair
                    for sub in range(ncols // 1024):
                        c0 = sub * 1024
                        spar = r_idx % 2

                        @block.scalar
                        def _(sc):
                            sc.wait_ge(ct.h, mm_done[oi])
                            if sub == 0:
                                sc.nop(cycle_cnt=2048)
                            if r_idx >= 2:
                                sc.wait_ge(cv.h, rope_cv[r_idx - 2])
                                sc.wait_ge(qs.h, rope_qs[r_idx - 2])
                            ca.inc(sc.copy(ev[spar][:], pkps[par][:, c0:c0 + 1024]))

                        @block.sync
                        def _(sync):
                            sync.wait_ge(ca.h, ca.n)
                            qs.inc(sync.dma_start(out=evsw[spar][0:64, :],
                                                  in_=ev[spar][64:128, :]), 16)
                            rope_qs.append(qs.inc(sync.dma_start(
                                out=evsw[spar][64:128, :], in_=ev[spar][0:64, :]), 16))

                        @block.vector
                        def _(v):
                            v.wait_ge(qs.h, rope_qs[r_idx])
                            if r_idx >= 2:
                                v.wait_ge(qa.h, rope_qa[r_idx - 2])
                            v.tensor_tensor(t1[:], ev[spar][:], cf[:, c0:c0 + 1024],
                                            ALU.mult)
                            vself.inc(v.tensor_tensor(t2[:], evsw[spar][:],
                                                      sf2[:, c0:c0 + 1024], ALU.mult))
                            v.wait_ge(vself.h, vself.n)
                            rope_cv.append(cv.inc(v.tensor_tensor(
                                kst[spar][:], t1[:], t2[:], ALU.add)))

                        @block.scalar
                        def _(sc):
                            sc.wait_ge(cv.h, rope_cv[r_idx])
                            rope_qa.append(qa.inc(sc.dma_start(
                                out=spec["dst"][:, spec["dst_col0"] + c0:
                                                spec["dst_col0"] + c0 + 1024],
                                in_=kst[spar][:]), 16))
                        r_idx += 1
                    evac_done.append((ca, ca.n))
                emit_dq(3)

        with contextlib.ExitStack() as es_dq:
            alloc_dq(es_dq)

            for chunk in range(2):
                with contextlib.ExitStack() as es_a:
                    ln_pass(es_a, f"k{chunk}",
                            x_full[chunk * 2048:(chunk + 1) * 2048, :], 2048,
                            ln1x_sp[chunk * 2048:(chunk + 1) * 2048, :])
                full_barrier()
                with contextlib.ExitStack() as es_a:
                    cs_k = [es_a.enter_context(
                        nc.sbuf_tensor(f"csk{chunk}{i}", [128, 2048], F32))
                        for i in range(2)]

                    @block.sync
                    def _(sync):
                        qs.inc(sync.dma_start(
                            out=cs_k[0][:],
                            in_=ck_c[:, chunk * 2048:(chunk + 1) * 2048]), 16)
                        qs.inc(sync.dma_start(
                            out=cs_k[1][:],
                            in_=ck_s2[:, chunk * 2048:(chunk + 1) * 2048]), 16)

                    of_specs = []
                    for kh in range(8):
                        of_specs.append(dict(kind="k", w_src=winkv_t[kh],
                                             dst=kf_sp[kh * 128:(kh + 1) * 128, :],
                                             dst_col0=chunk * 2048))
                    for vh in range(8):
                        of_specs.append(dict(kind="v", w_src=winkv_t[8 + vh],
                                             dst=vf_sp[vh * 128:(vh + 1) * 128, :],
                                             dst_col0=chunk * 2048))
                    proj_pass(es_a, f"kv{chunk}", ln1x_sp, chunk * 2048, 2048, of_specs,
                              (cs_k[0], cs_k[1]))
                full_barrier()

            with contextlib.ExitStack() as es_a:
                ln_pass(es_a, "q", x_own, 2048, ln1xq_sp)
            full_barrier()
            with contextlib.ExitStack() as es_a:
                cs_q = [es_a.enter_context(nc.sbuf_tensor(f"csq{i}", [128, 2048], F32))
                        for i in range(2)]

                @block.sync
                def _(sync):
                    qs.inc(sync.dma_start(out=cs_q[0][:], in_=cq_c[:]), 16)
                    qs.inc(sync.dma_start(out=cs_q[1][:], in_=cq_s2[:]), 16)

                of_specs = [dict(kind="q", w_src=winq_t[qh],
                                 dst=qf_sp[qh * 128:(qh + 1) * 128, :], dst_col0=0)
                            for qh in range(32)]
                proj_pass(es_a, "q", ln1xq_sp, 0, 2048, of_specs, (cs_q[0], cs_q[1]))
            full_barrier()

            finish_dq()
        full_barrier()

        if DEBUG_OUTPUTS:
            @block.sync
            def _(sync):
                sync.wait_ge(qa.h, qa.n)
                for nm, src in (("kf", kf_sp), ("vf", vf_sp), ("qf", qf_sp),
                                ("wt", wt_ffn), ("ln1", ln1x_sp), ("wob", wout_bf)):
                    qs.inc(sync.dma_start(out=dbg[nm][:, :], in_=src[:, :]), 16)

        a_spills_done = qa.n

        with contextlib.ExitStack() as es_b:
            masks_sb = es_b.enter_context(nc.sbuf_tensor("masks_sb", [128, 8 * 512],
                                                         BF16))
            khb = [es_b.enter_context(nc.sbuf_tensor(f"khb{i}", [128, S], BF16))
                   for i in range(2)]
            vhb = [es_b.enter_context(nc.sbuf_tensor(f"vhb{i}", [128, S], BF16))
                   for i in range(2)]
            qq = [es_b.enter_context(nc.sbuf_tensor(f"qq{i}", [128, 512], BF16))
                  for i in range(2)]
            ptq = [es_b.enter_context(nc.sbuf_tensor(f"ptq{i}", [128, 512], BF16))
                   for i in range(2)]
            lrow = es_b.enter_context(nc.sbuf_tensor("lrow", [1, 512], F32))
            lhi = es_b.enter_context(nc.sbuf_tensor("lhi", [1, 512], BF16))
            llo = es_b.enter_context(nc.sbuf_tensor("llo", [1, 512], BF16))
            lbcs = es_b.enter_context(nc.sbuf_tensor("lbcs", [128, 512], F32))
            avn = [es_b.enter_context(nc.sbuf_tensor(f"avn{i}", [128, 512], BF16))
                   for i in range(2)]
            sps = [es_b.enter_context(nc.psum_tensor(f"sps{i}", [128, 512], F32))
                   for i in range(2)]
            avps = es_b.enter_context(nc.psum_tensor("avps", [128, 512], F32))
            lps = es_b.enter_context(nc.psum_tensor("lps", [1, 512], F32))
            lbcps = es_b.enter_context(nc.psum_tensor("lbcps", [128, 512], F32))

            kv_loads = []
            head_done = []
            avn_spill = []
            avn_cv = []

            @block.sync
            def _(sync):
                sync.wait_ge(qa.h, a_spills_done)
                qs.inc(sync.dma_start(
                    out=masks_sb[:].rearrange("p (t q) -> p t q", t=8),
                    in_=masks.rearrange("t p q -> p t q")), 16)
                qs.inc(sync.dma_start(out=khb[0][:], in_=kf_sp[0:128, :]), 16)
                for kt in range(32):
                    qs.inc(sync.dma_start(out=vhb[0][:, kt * 128:(kt + 1) * 128],
                                          in_=vf_sp[0:128, kt * 128:(kt + 1) * 128],
                                          transpose=True), 16)
            kv_loads.append(qs.n)

            for kh in range(8):
                par = kh % 2

                if kh + 1 < 8:
                    @block.sync
                    def _(sync):
                        if kh >= 1:
                            sync.wait_ge(ct.h, head_done[-1])
                        qs.inc(sync.dma_start(
                            out=khb[(kh + 1) % 2][:],
                            in_=kf_sp[(kh + 1) * 128:(kh + 2) * 128, :]), 16)
                        for kt in range(32):
                            qs.inc(sync.dma_start(
                                out=vhb[(kh + 1) % 2][:, kt * 128:(kt + 1) * 128],
                                in_=vf_sp[(kh + 1) * 128:(kh + 2) * 128,
                                          kt * 128:(kt + 1) * 128],
                                transpose=True), 16)
                    kv_loads.append(qs.n)

                for g in range(4):
                    NT = 8 * (g + 1)
                    for qh in range(4):
                        qpar = qh % 2
                        hidx = len(head_done)
                        qload = []

                        @block.sync
                        def _(sync):
                            if hidx >= 2:
                                sync.wait_ge(ct.h, head_done[hidx - 2])
                            qload.append(qs.inc(sync.dma_start(
                                out=qq[qpar][:],
                                in_=qf_sp[(kh * 4 + qh) * 128:(kh * 4 + qh + 1) * 128,
                                          g * 512:(g + 1) * 512]), 16))

                        s_cnt = [None] * NT
                        exp_cnt = [None] * NT
                        gate_cnt = [None] * NT
                        lpv_cnt = [None] * NT

                        def emit_S(kt, NT=NT, qpar=qpar, par=par, kh=kh,
                                   qload=qload, s_cnt=s_cnt, exp_cnt=exp_cnt):
                            sp = kt % 2

                            @block.tensor
                            def _(t):
                                t.wait_ge(qs.h, max(qload[0], kv_loads[kh]))
                                if kt >= 2:
                                    t.wait_ge(ca.h, exp_cnt[kt - 2])
                                s_cnt[kt] = ct.inc(t.matmul(
                                    sps[sp][:], khb[par][:, kt * 128:(kt + 1) * 128],
                                    qq[qpar][:], start=True, stop=True))

                        def emit_exp(kt, NT=NT, s_cnt=s_cnt, exp_cnt=exp_cnt,
                                     gate_cnt=gate_cnt, lpv_cnt=lpv_cnt):
                            sp = kt % 2

                            @block.scalar
                            def _(sc):
                                sc.wait_ge(ct.h, s_cnt[kt])
                                if kt >= 2:
                                    sc.wait_ge(ct.h, lpv_cnt[kt - 2])
                                exp_cnt[kt] = ca.inc(sc.activation(
                                    out=ptq[sp][:], in_=sps[sp][:], func=AF.Exp,
                                    scale=INV_SQRT_D))

                            if kt >= NT - 8:
                                mi = kt - (NT - 8)

                                @block.vector
                                def _(v):
                                    v.wait_ge(ca.h, exp_cnt[kt])
                                    gate_cnt[kt] = ("cv", cv.inc(v.tensor_tensor(
                                        ptq[sp][:], ptq[sp][:],
                                        masks_sb[:, mi * 512:(mi + 1) * 512],
                                        ALU.mult)))
                            else:
                                gate_cnt[kt] = ("ca", exp_cnt[kt])

                        def emit_lpv(kt, NT=NT, par=par, gate_cnt=gate_cnt,
                                     lpv_cnt=lpv_cnt):
                            sp = kt % 2

                            @block.tensor
                            def _(t):
                                kind, val = gate_cnt[kt]
                                t.wait_ge((cv if kind == "cv" else ca).h, val)
                                if kt == 0 and avn_cv:
                                    t.wait_ge(cv.h, avn_cv[-1])
                                t.matmul(lps[:], ones_l[:], ptq[sp][:],
                                         start=(kt == 0), stop=(kt == NT - 1))
                                lpv_cnt[kt] = ct.inc(t.matmul(
                                    avps[:], vhb[par][:, kt * 128:(kt + 1) * 128],
                                    ptq[sp][:], start=(kt == 0), stop=(kt == NT - 1)))

                        emit_S(0)
                        emit_exp(0)
                        if NT > 1:
                            emit_S(1)
                            emit_exp(1)
                        for kt in range(NT):
                            emit_lpv(kt)
                            if kt + 2 < NT:
                                emit_S(kt + 2)
                                emit_exp(kt + 2)

                        @block.vector
                        def _(v):
                            v.wait_ge(ct.h, lpv_cnt[NT - 1])
                            v.nop(cycle_cnt=512)
                            vself.inc(v.reciprocal(lrow[:], lps[:]))
                            v.wait_ge(vself.h, vself.n)
                            vself.inc(v.tensor_copy(lhi[:], lrow[:]))
                            v.wait_ge(vself.h, vself.n)
                            cv.inc(v.tensor_tensor(llo[:], lrow[:], lhi[:],
                                                   ALU.subtract))

                        @block.tensor
                        def _(t):
                            t.wait_ge(cv.h, cv.n)
                            t.matmul(lbcps[:], ones_b[:], lhi[:], start=True, stop=False)
                            ct.inc(t.matmul(lbcps[:], ones_b[:], llo[:],
                                            start=False, stop=True))

                        @block.scalar
                        def _(sc):
                            sc.wait_ge(ct.h, ct.n)
                            sc.nop(cycle_cnt=512)
                            ca.inc(sc.copy(lbcs[:], lbcps[:]))

                        @block.vector
                        def _(v):
                            v.wait_ge(ca.h, ca.n)
                            v.nop(cycle_cnt=512)
                            if hidx >= 2:
                                v.wait_ge(qa.h, avn_spill[hidx - 2])
                            avn_cv.append(cv.inc(v.tensor_tensor(
                                avn[qpar][:], avps[:], lbcs[:], ALU.mult)))

                        @block.scalar
                        def _(sc):
                            sc.wait_ge(cv.h, avn_cv[-1])
                            avn_spill.append(qa.inc(sc.dma_start(
                                out=avf_sp[(kh * 4 + qh) * 128:(kh * 4 + qh + 1) * 128,
                                           g * 512:(g + 1) * 512],
                                in_=avn[qpar][:]), 16))
                        head_done.append(ct.n)

            b_spills_done = qa.n

        if DEBUG_OUTPUTS:
            @block.sync
            def _(sync):
                sync.wait_ge(qa.h, qa.n)
                qs.inc(sync.dma_start(out=dbg["avf"][:, :], in_=avf_sp[:, :]), 16)

        full_barrier()
        with contextlib.ExitStack() as es_c:
            avf = es_c.enter_context(nc.sbuf_tensor("avf", [128, 32 * 2048], BF16))
            zrh = es_c.enter_context(nc.sbuf_tensor("zrh", [128, 512], BF16))

            @block.vector
            def _(v):
                v.memset(zrh[:], 0.0)
                cv.inc(v.tensor_copy(zrh[:, 0:1], zrh[:, 0:1]))
            zr_ready_c = cv.n

            @block.sync
            def _(sync):
                sync.wait_ge(qa.h, b_spills_done)
                qs.inc(sync.dma_start(
                    out=avf[:].rearrange("p (g t) -> p g t", g=32),
                    in_=avf_sp.rearrange("(g p) t -> p g t", p=128)), 16)
            avf_loaded = qs.n

            wjt = [es_c.enter_context(nc.sbuf_tensor(f"wjt{i}", [128, 32 * 128], BF16))
                   for i in range(2)]
            ctmp = es_c.enter_context(nc.sbuf_tensor("ctmp", [128, 2048], F32))
            chi = [es_c.enter_context(nc.sbuf_tensor(f"chi{i}", [128, 2048], BF16))
                   for i in range(2)]
            clo = [es_c.enter_context(nc.sbuf_tensor(f"clo{i}", [128, 2048], BF16))
                   for i in range(2)]
            cps = [es_c.enter_context(nc.psum_tensor(f"cps{i}", [128, 2048], F32))
                   for i in range(2)]

            wj_loads = []
            mmc = []
            evc = []
            spc = []

            @block.scalar
            def _(sc):
                sc.wait_ge(qg.h, woutbf_done)
                wj_loads.append(qa.inc(sc.dma_start(
                    out=wjt[0][:].rearrange("p (g o) -> p g o", g=32),
                    in_=wout_bf[:, 0:128].rearrange("(g p) o -> p g o", p=128)), 16))

            for j in range(32):
                par = j % 2

                if j + 1 < 32:
                    @block.scalar
                    def _(sc):
                        if j >= 1:
                            sc.wait_ge(ct.h, mmc[j - 1])
                        wj_loads.append(qa.inc(sc.dma_start(
                            out=wjt[(j + 1) % 2][:].rearrange("p (g o) -> p g o", g=32),
                            in_=wout_bf[:, (j + 1) * 128:(j + 2) * 128].rearrange(
                                "(g p) o -> p g o", p=128)), 16))

                @block.tensor
                def _(t):
                    t.wait_ge(qa.h, wj_loads[j])
                    t.wait_ge(qs.h, avf_loaded)
                    t.wait_ge(cv.h, zr_ready_c)
                    if j >= 2:
                        t.wait_ge(ca.h, evc[j - 2])
                    for kx in range(4):
                        t.matmul(cps[par][:, kx * 512:(kx + 1) * 512],
                                 wjt[par][:, 0:128], zrh[:], start=True, stop=False)
                    ins = None
                    for e in range(32):
                        for kx in range(4):
                            ins = t.matmul(
                                cps[par][:, kx * 512:(kx + 1) * 512],
                                wjt[par][:, e * 128:(e + 1) * 128],
                                avf[:, e * 2048 + kx * 512: e * 2048 + (kx + 1) * 512],
                                start=(e == 0), stop=(e == 31))
                    mmc.append(ct.inc(ins))

                @block.scalar
                def _(sc):
                    sc.wait_ge(ct.h, mmc[j])
                    if j >= 2:
                        sc.wait_ge(cv.h, spc[j - 2][0])
                        sc.wait_ge(qa.h, spc[j - 2][1])
                    evc.append(ca.inc(sc.copy(ctmp[:], cps[par][:])))

                @block.vector
                def _(v):
                    v.wait_ge(ca.h, evc[j])
                    v.tensor_copy(chi[par][:], ctmp[:])
                    vself.inc(v.tensor_copy(chi[par][:, 0:1], chi[par][:, 0:1]))
                    v.wait_ge(vself.h, vself.n)
                    cv.inc(v.tensor_tensor(clo[par][:], ctmp[:], chi[par][:],
                                           ALU.subtract))

                @block.scalar
                def _(sc):
                    sc.wait_ge(cv.h, cv.n)
                    qa.inc(sc.dma_start(out=at_hi_sp[j * 128:(j + 1) * 128, :],
                                        in_=chi[par][:]), 16)
                    spc.append((cv.n, qa.inc(sc.dma_start(
                        out=at_lo_sp[j * 128:(j + 1) * 128, :], in_=clo[par][:]), 16)))

        full_barrier()
        with contextlib.ExitStack() as es_cr:
            rhi = [es_cr.enter_context(nc.sbuf_tensor(f"rhi{i}", [128, E], BF16))
                   for i in range(2)]
            rlo = [es_cr.enter_context(nc.sbuf_tensor(f"rlo{i}", [128, E], BF16))
                   for i in range(2)]
            xr = [es_cr.enter_context(nc.sbuf_tensor(f"xr{i}", [128, E], F32))
                  for i in range(2)]
            r1t = [es_cr.enter_context(nc.sbuf_tensor(f"r1t{i}", [128, E], F32))
                   for i in range(2)]
            tl = []
            xl = []
            r1d = []
            r1s = []

            @block.sync
            def _(sync):
                for i in range(2):
                    qs.inc(sync.dma_start(out=rhi[i][:],
                                          in_=at_hi_sp[:, i * 128:(i + 1) * 128],
                                          transpose=True), 16)
                    qs.inc(sync.dma_start(out=rlo[i][:],
                                          in_=at_lo_sp[:, i * 128:(i + 1) * 128],
                                          transpose=True), 16)
                    xl.append(qs.inc(sync.dma_start(
                        out=xr[i][:], in_=x_own[i * 128:(i + 1) * 128, :]), 16))
                tl.append(qs.n)

            for tt in range(16):
                par = tt % 2

                @block.vector
                def _(v):
                    v.wait_ge(qs.h, tl[0] if tt < 2 else tl[tt - 1])
                    if tt >= 2:
                        v.wait_ge(qa.h, r1s[tt - 2])
                    v.tensor_tensor(r1t[par][:], rhi[par][:], rlo[par][:], ALU.add)
                    vself.inc(v.tensor_copy(r1t[par][:, 0:1], r1t[par][:, 0:1]))
                    v.wait_ge(vself.h, vself.n)
                    r1d.append(cv.inc(v.tensor_tensor(r1t[par][:], r1t[par][:],
                                                      xr[par][:], ALU.add)))

                @block.scalar
                def _(sc):
                    sc.wait_ge(cv.h, r1d[tt])
                    r1s.append(qa.inc(sc.dma_start(
                        out=r1_sp[tt * 128:(tt + 1) * 128, :], in_=r1t[par][:]), 16))

                if tt + 2 < 16:
                    @block.sync
                    def _(sync):
                        ttn = tt + 2
                        sync.wait_ge(cv.h, r1d[tt])
                        qs.inc(sync.dma_start(out=rhi[par][:],
                                              in_=at_hi_sp[:, ttn * 128:(ttn + 1) * 128],
                                              transpose=True), 16)
                        qs.inc(sync.dma_start(out=rlo[par][:],
                                              in_=at_lo_sp[:, ttn * 128:(ttn + 1) * 128],
                                              transpose=True), 16)
                        xl.append(qs.inc(sync.dma_start(
                            out=xr[par][:], in_=x_own[ttn * 128:(ttn + 1) * 128, :]), 16))
                        tl.append(qs.n)

        full_barrier()
        with contextlib.ExitStack() as es_c2:
            ln_pass(es_c2, "c2", r1_sp, 2048, ln2x_sp, perm_out=True)
        full_barrier()
        ln2_spills_done = qa.n

        if DEBUG_OUTPUTS:
            @block.sync
            def _(sync):
                sync.wait_ge(qa.h, qa.n)
                qs.inc(sync.dma_start(out=dbg["r1"][:, :], in_=r1_sp[:, :]), 16)
                qs.inc(sync.dma_start(out=dbg["ln2"][:, :], in_=ln2x_sp[:, :]), 16)

        with contextlib.ExitStack() as es_d:
            l2f = es_d.enter_context(nc.sbuf_tensor("l2f", [128, 32 * 2048], BF16))
            zrd = es_d.enter_context(nc.sbuf_tensor("zrd", [128, 512], BF16))

            @block.vector
            def _(v):
                v.memset(zrd[:], 0.0)
                cv.inc(v.tensor_copy(zrd[:, 0:1], zrd[:, 0:1]))
            zr_ready = cv.n

            @block.sync
            def _(sync):
                sync.wait_ge(qa.h, ln2_spills_done)
                for jj in range(32):
                    qs.inc(sync.dma_start(
                        out=l2f[:, jj * 2048:(jj + 1) * 2048],
                        in_=ln2x_sp[:, jj * 128:(jj + 1) * 128], transpose=True), 16)
            l2f_loaded = qs.n

            fjt = [es_d.enter_context(nc.sbuf_tensor(f"fjt{i}", [128, 32 * 128], BF16))
                   for i in range(2)]
            dtmp = es_d.enter_context(nc.sbuf_tensor("dtmp", [128, 2048], F32))
            dhi = [es_d.enter_context(nc.sbuf_tensor(f"dhi{i}", [128, 2048], BF16))
                   for i in range(2)]
            dlo = [es_d.enter_context(nc.sbuf_tensor(f"dlo{i}", [128, 2048], BF16))
                   for i in range(2)]
            dps = [es_d.enter_context(nc.psum_tensor(f"dps{i}", [128, 2048], F32))
                   for i in range(2)]

            fj_loads = []
            mmd = []
            evd = []
            spd = []
            dq_all_stored = max(v for v in wt_slab_ready.values())

            @block.scalar
            def _(sc):
                sc.wait_ge(qa.h, dq_all_stored)
                fj_loads.append(qa.inc(sc.dma_start(
                    out=fjt[0][:].rearrange("p (g o) -> p g o", g=32),
                    in_=wt_ffn[:, 0:128].rearrange("(g p) o -> p g o", p=128)), 16))

            for j in range(32):
                par = j % 2

                if j + 1 < 32:
                    @block.scalar
                    def _(sc):
                        if j >= 1:
                            sc.wait_ge(ct.h, mmd[j - 1])
                        fj_loads.append(qa.inc(sc.dma_start(
                            out=fjt[(j + 1) % 2][:].rearrange("p (g o) -> p g o", g=32),
                            in_=wt_ffn[:, (j + 1) * 128:(j + 2) * 128].rearrange(
                                "(g p) o -> p g o", p=128)), 16))

                @block.tensor
                def _(t):
                    t.wait_ge(qa.h, fj_loads[j])
                    t.wait_ge(qs.h, l2f_loaded)
                    t.wait_ge(cv.h, zr_ready)
                    if j >= 2:
                        t.wait_ge(ca.h, evd[j - 2])
                    for kx in range(4):
                        t.matmul(dps[par][:, kx * 512:(kx + 1) * 512],
                                 fjt[par][:, 0:128], zrd[:], start=True, stop=False)
                    ins = None
                    for p in range(32):
                        for kx in range(4):
                            ins = t.matmul(
                                dps[par][:, kx * 512:(kx + 1) * 512],
                                fjt[par][:, p * 128:(p + 1) * 128],
                                l2f[:, p * 2048 + kx * 512: p * 2048 + (kx + 1) * 512],
                                start=(p == 0), stop=(p == 31))
                    mmd.append(ct.inc(ins))

                @block.scalar
                def _(sc):
                    sc.wait_ge(ct.h, mmd[j])
                    if j >= 2:
                        sc.wait_ge(cv.h, spd[j - 2][0])
                        sc.wait_ge(qa.h, spd[j - 2][1])
                    evd.append(ca.inc(sc.copy(dtmp[:], dps[par][:])))

                @block.vector
                def _(v):
                    v.wait_ge(ca.h, evd[j])
                    v.tensor_copy(dhi[par][:], dtmp[:])
                    vself.inc(v.tensor_copy(dhi[par][:, 0:1], dhi[par][:, 0:1]))
                    v.wait_ge(vself.h, vself.n)
                    cv.inc(v.tensor_tensor(dlo[par][:], dtmp[:], dhi[par][:],
                                           ALU.subtract))

                @block.scalar
                def _(sc):
                    sc.wait_ge(cv.h, cv.n)
                    qa.inc(sc.dma_start(out=ff_hi_sp[j * 128:(j + 1) * 128, :],
                                        in_=dhi[par][:]), 16)
                    spd.append((cv.n, qa.inc(sc.dma_start(
                        out=ff_lo_sp[j * 128:(j + 1) * 128, :], in_=dlo[par][:]), 16)))

        full_barrier()
        with contextlib.ExitStack() as es_df:
            fhi = [es_df.enter_context(nc.sbuf_tensor(f"fhi{i}", [128, E], BF16))
                   for i in range(2)]
            flo = [es_df.enter_context(nc.sbuf_tensor(f"flo{i}", [128, E], BF16))
                   for i in range(2)]
            r1b = [es_df.enter_context(nc.sbuf_tensor(f"r1b{i}", [128, E], F32))
                   for i in range(2)]
            obuf = [es_df.enter_context(nc.sbuf_tensor(f"obuf{i}", [128, E], F32))
                    for i in range(2)]
            biasbc_sb = es_df.enter_context(nc.sbuf_tensor("biasbc_sb", [128, E], F32))
            tl = []
            od = []
            os_ = []

            @block.sync
            def _(sync):
                qs.inc(sync.dma_start(out=biasbc_sb[:], in_=biasbc[:]), 16)
                for i in range(2):
                    qs.inc(sync.dma_start(out=fhi[i][:],
                                          in_=ff_hi_sp[:, i * 128:(i + 1) * 128],
                                          transpose=True), 16)
                    qs.inc(sync.dma_start(out=flo[i][:],
                                          in_=ff_lo_sp[:, i * 128:(i + 1) * 128],
                                          transpose=True), 16)
                    qs.inc(sync.dma_start(out=r1b[i][:],
                                          in_=r1_sp[i * 128:(i + 1) * 128, :]), 16)
                tl.append(qs.n)

            for tt in range(16):
                par = tt % 2

                @block.vector
                def _(v):
                    v.wait_ge(qs.h, tl[0] if tt < 2 else tl[tt - 1])
                    if tt >= 2:
                        v.wait_ge(st.h, os_[tt - 2])
                    v.tensor_tensor(obuf[par][:], fhi[par][:], flo[par][:], ALU.add)
                    vself.inc(v.tensor_copy(obuf[par][:, 0:1], obuf[par][:, 0:1]))
                    v.wait_ge(vself.h, vself.n)
                    v.tensor_tensor(obuf[par][:], obuf[par][:], biasbc_sb[:], ALU.add)
                    vself.inc(v.tensor_copy(obuf[par][:, 0:1], obuf[par][:, 0:1]))
                    v.wait_ge(vself.h, vself.n)
                    od.append(cv.inc(v.tensor_tensor(obuf[par][:], obuf[par][:],
                                                     r1b[par][:], ALU.add)))

                @block.scalar
                def _(sc):
                    sc.wait_ge(cv.h, od[tt])
                    os_.append(st.inc(sc.dma_start(
                        out=out_own[tt * 128:(tt + 1) * 128, :], in_=obuf[par][:]), 16))

                if tt + 2 < 16:
                    @block.sync
                    def _(sync):
                        ttn = tt + 2
                        sync.wait_ge(cv.h, od[tt])
                        qs.inc(sync.dma_start(out=fhi[par][:],
                                              in_=ff_hi_sp[:, ttn * 128:(ttn + 1) * 128],
                                              transpose=True), 16)
                        qs.inc(sync.dma_start(out=flo[par][:],
                                              in_=ff_lo_sp[:, ttn * 128:(ttn + 1) * 128],
                                              transpose=True), 16)
                        qs.inc(sync.dma_start(out=r1b[par][:],
                                              in_=r1_sp[ttn * 128:(ttn + 1) * 128, :]), 16)
                        tl.append(qs.n)

            @block.sync
            def _(sync):
                sync.wait_ge(st.h, st.n)

    return nc, dbg


_cached = {}


def get_program():
    key = ("prog", DEBUG_OUTPUTS)
    if key not in _cached:
        _cached[key] = build_program()
    return _cached[key]



def host_inputs(L0_attn_x, L0_attn_w_in, L0_attn_w_out, L0_ffn_weight_packed,
                L0_ffn_scale, L0_ffn_bias):
    x = np.asarray(L0_attn_x, dtype=np.float32)
    w_in = np.asarray(L0_attn_w_in, dtype=np.float32)
    w_out = np.asarray(L0_attn_w_out, dtype=np.float32)
    pk = np.asarray(L0_ffn_weight_packed, dtype=np.int32)
    scale = np.asarray(L0_ffn_scale, dtype=np.float32)
    bias = np.asarray(L0_ffn_bias, dtype=np.float32)

    w_inT = np.ascontiguousarray(w_in.T)
    winq_t = np.ascontiguousarray(
        w_inT[:, 0:4096].reshape(E, 32, 128).transpose(1, 0, 2))
    winkv_t = np.ascontiguousarray(
        w_inT[:, 4096:6144].reshape(E, 16, 128).transpose(1, 0, 2))
    woutT = np.ascontiguousarray(w_out.T)
    pkT = np.ascontiguousarray(pk.reshape(4096, 2048).T)
    srep = np.ascontiguousarray(np.repeat(scale.reshape(4096, 64).T, 32, axis=0))
    biasbc = np.ascontiguousarray(np.broadcast_to(bias, (128, E)))

    inv_freq = 1.0 / (10000.0 ** (np.arange(0, 128, 2, dtype=np.float64) / 128.0))
    ang = np.arange(S, dtype=np.float64)[:, None] * inv_freq[None, :]
    cosT = np.cos(ang).T
    sinT = np.sin(ang).T
    ck_c = np.ascontiguousarray(np.concatenate([cosT, cosT], 0).astype(np.float32))
    ck_s2 = np.ascontiguousarray(np.concatenate([-sinT, sinT], 0).astype(np.float32))

    shared = dict(winq_t=winq_t, winkv_t=winkv_t, woutT=woutT, pkT=pkT,
                  srep=srep, biasbc=biasbc, ck_c=ck_c, ck_s2=ck_s2)

    in_maps = []
    for c in range(8):
        b, h = c // 2, c % 2
        rows = own_row_index(h)
        kk = np.arange(128)[:, None]
        qq_ = np.arange(512)[None, :]
        m = np.zeros((8, 128, 512), dtype=np.float32)
        for t in range(8):
            m[t] = (128 * t + kk <= 512 * h + qq_).astype(np.float32)
        im = dict(shared)
        im["x_full"] = np.ascontiguousarray(x[b])
        im["x_own"] = np.ascontiguousarray(x[b][rows])
        im["cq_c"] = np.ascontiguousarray(ck_c[:, rows])
        im["cq_s2"] = np.ascontiguousarray(ck_s2[:, rows])
        im["masks"] = m.astype(ml_dtypes.bfloat16)
        in_maps.append(im)
    return in_maps


def kernel(L0_attn_x, L0_attn_w_in, L0_attn_w_out, L0_ffn_weight_packed,
           L0_ffn_scale, L0_ffn_bias):
    in_maps = host_inputs(L0_attn_x, L0_attn_w_in, L0_attn_w_out,
                          L0_ffn_weight_packed, L0_ffn_scale, L0_ffn_bias)
    nc, _dbg = get_program()
    res = run_bass_kernel_spmd(nc, in_maps, core_ids=list(range(8)))
    out = np.empty((Bsz, S, E), dtype=np.float32)
    for c in range(8):
        b, h = c // 2, c % 2
        rows = own_row_index(h)
        out[b][rows] = np.asarray(res.results[c]["out_own"])
    return out
